# version 01 (final)
# baseline (speedup 1.0000x reference)
import numpy as np
import ml_dtypes
from contextlib import ExitStack
import concourse.bass as bass
import concourse.mybir as mybir
from concourse.bass_utils import run_bass_kernel_spmd

F32 = mybir.dt.float32
BF16 = mybir.dt.bfloat16
AF = mybir.ActivationFunctionType
ALU = mybir.AluOpType
AX = mybir.AxisListType

NCORES = 8
D = 1024
TLOC = 2048
TB = 512
NTB = 4
NCD = 8
NTC = 16
HEADS = 16
TSEQ = 4096
LN_EPS = 1e-5
THRESH = 0.29514
SHARP = 15.0

KBE = 1024 * 2048
VBE = 2048 * 1040
NBE = 16 * 4096
CC1 = KBE + VBE + NBE

ENGINES = ("tensor", "scalar", "vector", "gpsimd", "sync")

STOP_AFTER = None


class MK:
    def __init__(self, nc):
        self.nc = nc
        self.items = {e: [] for e in ENGINES}
        self.semh = {}
        self.cnt = {}
        self.hiwater = {e: {} for e in ENGINES}

    def add_sem(self, name, handle):
        self.semh[name] = handle
        self.cnt[name] = 0

    def op(self, eng, fn, waits=(), inc=None, inc_by=1, strict=False):
        merged = {}
        for t in waits:
            if t is None:
                continue
            s, v, peng = t
            assert v <= self.cnt[s], f"future token {s}={v} (now {self.cnt[s]})"
            if peng == eng and peng is not None and not strict:
                continue
            hw = self.hiwater[eng].get(s, 0)
            if v <= hw:
                continue
            self.hiwater[eng][s] = v
            merged[s] = max(merged.get(s, 0), v)
        tok = None
        if inc is not None:
            self.cnt[inc] += inc_by
            tok = (inc, self.cnt[inc], eng if inc_by == 1 else None)
        self.items[eng].append((merged, fn, inc, inc_by))
        return tok

    def dma(self, q, out, in_, waits=()):
        sem = {"sync": "dsy", "gpsimd": "dgp", "scalar": "dsc"}[q]
        return self.op(q, lambda e: e.dma_start(out=out, in_=in_), waits=waits,
                       inc=sem, inc_by=16)

    def mm(self, out, lhsT, rhs, start, stop, waits=(), pub=False, tp=None):
        return self.op(
            "tensor",
            lambda e: e.matmul(out, lhsT, rhs, start=start, stop=stop,
                               tile_position=tp),
            waits=waits, inc="pe" if pub else None)

    def transp(self, out, in_, ident, waits=(), pub=False):
        return self.op("tensor", lambda e: e.transpose(out, in_, ident),
                       waits=waits, inc="pe" if pub else None)

    act_extra = ()

    def act(self, out, in_, func, bias=0.0, scale=1.0, waits=(), pub=True):
        return self.op(
            "scalar",
            lambda e: e.activation(out, in_, func, bias=bias, scale=scale),
            waits=list(waits) + list(self.act_extra),
            inc="se" if pub else None)

    def vop(self, fn, waits=(), pub=True, strict=False):
        return self.op("vector", fn, waits=waits, inc="ve" if pub else None,
                       strict=strict)

    def gop(self, fn, waits=(), pub=True):
        return self.op("gpsimd", fn, waits=waits, inc="ge" if pub else None)

    def ccop(self, kind, alu, ins, outs, waits=()):
        tok = self.op(
            "gpsimd",
            lambda e: e.collective_compute(
                kind, alu, replica_groups=[list(range(NCORES))],
                ins=ins, outs=outs),
            waits=waits, inc="cc", inc_by=1)
        return (tok[0], tok[1], None)

    def emit(self, block):
        def make(eng_name):
            def body(engine):
                for merged, fn, inc, inc_by in self.items[eng_name]:
                    for s, v in merged.items():
                        engine.wait_ge(self.semh[s], v)
                    if fn is not None:
                        instr = fn(engine)
                        if inc is not None:
                            instr.then_inc(self.semh[inc], inc_by)
            return body
        for e in ENGINES:
            getattr(block, e)(make(e))


class Buf:
    def __init__(self, h, off, cols):
        self.h = h
        self.off = off
        self.cols = cols

    def __getitem__(self, idx):
        p, c = idx
        a = 0 if c.start is None else c.start
        b = self.cols if c.stop is None else c.stop
        assert 0 <= a <= b <= self.cols, (a, b, self.cols)
        return self.h[p, self.off + a:self.off + b]

    def ap(self):
        return self.h[:, self.off:self.off + self.cols]

    def sub(self, off, cols):
        assert off + cols <= self.cols, (off, cols, self.cols)
        return Buf(self.h, self.off + off, cols)


def _gset(i):
    return list(range(0, i + 1)) + list(range(4, 4 + i + 1))


class Ctx:
    pass


def sm(C, r):
    r0, r1, c0, c1 = r
    return C.SM[r0:r1, c0:c1]


QN = (0, 16, 0, 2048)
KN = (32, 48, 0, 2048)
KNATMP = (0, 16, 2048, 4096)
GKV = (0, 16, 0, 2048)
KRN = (0, 16, 2048, 2560)
RD_ = (0, 128, 0, 1024)
RC_ = (0, 128, 1024, 1536)
ST16 = (0, 16, 3632, 3648)
ROWA = (32, 33, 2592, 3104)
ROWB = (32, 33, 3104, 3616)
DROW = (64, 65, 2080, 2592)
DROWO = (0, 1, 2080, 2592)
SCOL = (0, 128, 3616, 3632)
TACC = (0, 128, 3632, 3648)
TRED = (0, 128, 3648, 3664)
GKVT = (0, 128, 3696, 3952)
LNS = (0, 128, 3952, 3968)


def build_program(taps=()):
    nc = bass.Bass("TRN2", target_bir_lowering=False)
    C = Ctx()
    C.nc = nc
    C.taps_req = set(taps)
    C.taps = {}

    def din(name, shape, dtype=F32):
        t = nc.dram_tensor(name, list(shape), dtype, kind="ExternalInput")
        setattr(C, name, t)
        return t

    din("xT", (D, TLOC))
    din("qkvT", (D, 3 * D))
    din("projT", (D, D))
    din("phiT", (D, 2048))
    din("phib_row", (1, 2048))
    din("expw", (8, D, D))
    din("expb_dev", (4, 64, D))
    din("wq", (D, D)); din("wk", (D, D)); din("wv", (D, D)); din("wo", (D, D))
    din("bo_c", (128, NCD))
    din("lnw_c", (128, NCD)); din("lnb_c", (128, NCD))
    din("w1T", (D, 4 * D)); din("b1_c", (128, 32))
    din("w2T", (4 * D, D)); din("b2_c", (128, NCD))
    din("masks_in", (4, 8, 128, TB), BF16)
    din("ident_b", (128, 128), BF16)
    din("ident_f", (128, 128), F32)
    din("headpat", (128, 8 * 16))
    din("gqpat", (128, 8 * 16), BF16)
    din("gkpat", (128, 8 * 16), BF16)
    din("reppatF", (16, 8 * 128))
    din("ones_r", (128, 128))
    din("gqb", (128, 256), BF16)
    din("ones_rb", (1, 128), BF16)
    din("ones_c", (128, 1))
    din("ones_cb", (128, 1), BF16)

    C.out_xT = nc.dram_tensor("out_xT", [D, TLOC], F32, kind="ExternalOutput")

    C.cc1_in = nc.dram_tensor("cc1_in", [CC1], BF16)
    C.cc1_out = nc.dram_tensor("cc1_out", [NCORES * CC1], BF16)
    C.pair1 = nc.dram_tensor("pair1", [2 * CC1], BF16)
    C.rs_in = nc.dram_tensor("rs_in", [2048, D], F32)
    C.rs_out = nc.dram_tensor("rs_out", [256, D], F32)
    C.ag2_in = nc.dram_tensor("ag2_in", [256, D], BF16)
    C.ag2_out = nc.dram_tensor("ag2_out", [2048, D], BF16)
    C.w1q_d = nc.dram_tensor("w1q_d", [D, 4 * D], BF16)
    C.w2q_d = nc.dram_tensor("w2q_d", [4 * D, D], BF16)

    mk = MK(nc)
    C.mk = mk
    es = ExitStack()
    with es:
        for s in ("pe", "se", "ve", "ge", "dsy", "dgp", "dsc", "cc"):
            mk.add_sem(s, es.enter_context(nc.semaphore(s)))

        def sbt(name, p, f, dtype):
            t = es.enter_context(nc.sbuf_tensor(name, [p, f], dtype))
            setattr(C, name, t)
            return t

        sbt("IDB", 128, 128, BF16)
        sbt("IDF", 128, 128, F32)
        sbt("HPAT", 128, 128, F32)
        sbt("GQPAT", 128, 128, BF16)
        sbt("GKPAT", 128, 128, BF16)
        sbt("RPF", 16, 1024, F32)
        sbt("ONER", 128, 128, F32)
        sbt("GQB", 128, 256, BF16)
        sbt("ONERB", 1, 128, BF16)
        sbt("ONEC", 128, 1, F32)
        sbt("ONECB", 128, 1, BF16)
        sbt("PHIBS", 1, 2048, BF16)
        sbt("BIASC", 128, 64, F32)
        sbt("SM", 128, 4096, F32)
        sbt("CVAL", 128, 4, F32)

        C.PSB = es.enter_context(nc.psum_tensor("PSB", [128, 2048], F32))
        C.PS4 = es.enter_context(nc.psum_tensor("PS4", [128, 512], F32))
        C.PS5 = es.enter_context(nc.psum_tensor("PS5", [128, 512], F32))
        C.PS6 = es.enter_context(nc.psum_tensor("PS6", [128, 512], F32))
        C.PS7 = es.enter_context(nc.psum_tensor("PS7", [128, 512], F32))
        C.PS4B = C.PS4.bitcast(BF16)

        C.XS = sbt("XS", 128, NCD * TLOC, F32)

        AR = es.enter_context(nc.sbuf_tensor("AR", [128, 57728], BF16))
        C.S0 = Buf(AR, 0, 16384)
        C.S1 = Buf(AR, 16384, 16384)
        C.E64 = Buf(AR, 0, 32768)
        C.S2 = Buf(AR, 32768, 16576)
        C.SCR = Buf(AR, 49536, 8192)
        ARF = AR.bitcast(F32)
        C.S2F = Buf(ARF, 16384, 8288)
        C.SCRF = Buf(ARF, 24768, 4096)

        block = es.enter_context(nc.Block())

        mk.dma("sync", C.IDB[:, :], C.ident_b[:, :])
        mk.dma("sync", C.IDF[:, :], C.ident_f[:, :])
        mk.dma("sync", C.HPAT[:, :], C.headpat[:, :])
        mk.dma("gpsimd", C.GQPAT[:, :], C.gqpat[:, :])
        mk.dma("gpsimd", C.GKPAT[:, :], C.gkpat[:, :])
        mk.dma("sync", C.RPF[:, :], C.reppatF[:, :])
        mk.dma("sync", C.ONER[:, :], C.ones_r[:, :])
        mk.dma("gpsimd", C.GQB[:, :], C.gqb[:, :])
        mk.dma("sync", C.ONERB[:, :], C.ones_rb[:, :])
        mk.dma("sync", C.ONEC[:, :], C.ones_c[:, :])
        mk.dma("sync", C.ONECB[:, :], C.ones_cb[:, :])
        mk.dma("gpsimd", C.PHIBS[:, :], C.phib_row[:, :])
        mk.dma("sync", C.BIASC[:, 0:8], C.bo_c[:, :])
        mk.dma("sync", C.BIASC[:, 8:16], C.lnw_c[:, :])
        mk.dma("sync", C.BIASC[:, 16:24], C.lnb_c[:, :])
        mk.dma("sync", C.BIASC[:, 24:32], C.b2_c[:, :])
        mk.dma("sync", C.BIASC[:, 32:64], C.b1_c[:, :])
        C.tok_x = mk.dma(
            "sync", C.XS.ap().rearrange("p (c t) -> p c t", t=TLOC),
            C.xT.ap().rearrange("(c p) t -> c p t", p=128).transpose([1, 0, 2]))
        t_cv = None
        act_consts = (LN_EPS, 0.125, SHARP, -THRESH * SHARP)
        for j, v in enumerate(act_consts):
            t_cv = mk.vop(lambda e, jj=j, vv=v: e.memset(
                C.CVAL[:, jj:jj + 1], vv), pub=(j == len(act_consts) - 1))
            nc.const_aps.register(F32, v, C.CVAL[:, j:j + 1])
        mk.act_extra = (t_cv,)
        C.tok_consts = [("dsy", mk.cnt["dsy"], None),
                        ("dgp", mk.cnt["dgp"], None), t_cv]

        build_forward(C)

        t_out = mk.dma(
            "sync",
            C.out_xT.ap().rearrange("(c p) t -> c p t", p=128).transpose([1, 0, 2]),
            C.XS.ap().rearrange("p (c t) -> p c t", t=TLOC),
            waits=C.tok_stream)
        for name in sorted(C.taps_req):
            if name not in C.taps:
                continue
            ap, toks = C.taps[name]
            o = nc.dram_tensor(f"tap_{name}", list(ap.shape), ap.dtype,
                               kind="ExternalOutput")
            mk.dma("sync", o.ap(), ap, waits=toks)
        mk.op("sync", None, waits=[t_out])
        fin = dict(mk.cnt)
        for s in ("dsy", "dgp", "dsc", "cc"):
            if fin[s]:
                mk.op("sync", None, waits=[(s, fin[s], None)])
        mk.emit(block)
    return nc


def tap(C, name, ap, toks):
    if name in C.taps_req:
        C.taps[name] = (ap, list(toks))


def XSc(C, c, tb=None):
    if tb is None:
        return C.XS[:, 2048 * c:2048 * (c + 1)]
    return C.XS[:, 2048 * c + TB * tb:2048 * c + TB * (tb + 1)]


def slot_c(S, c, tb=None):
    if tb is None:
        return S[:, 2048 * c:2048 * (c + 1)]
    return S[:, 2048 * c + TB * tb:2048 * c + TB * (tb + 1)]


def ln_feat(C, src_chunk, dst_chunk, waits, src_f32=True, lnwb=False):
    mk = C.mk
    onesc = C.ONEC[:, 0:1] if src_f32 else C.ONECB[:, 0:1]
    ev = [C.SCRF.sub(3072, 512), C.SCRF.sub(3584, 512)]
    evo = C.SCRF.sub(2560, 512)
    rowm = sm(C, ROWA)
    rowv = sm(C, ROWB)
    out_toks = []
    prev_apply = []
    prev_bc = []
    sq_busy = [None, None]
    for tb in range(NTB):
        t_s = t_q = None
        for c in range(NCD):
            src = src_chunk(c, tb)
            w0 = list(waits) + (prev_apply if c == 0 else [])
            t_s = mk.mm(C.PS6[32:33, :], onesc, src, start=(c == 0),
                        stop=(c == NCD - 1), waits=w0, pub=(c == NCD - 1),
                        tp=(0, 32))
            t_sq = mk.vop(lambda e, s=src, d=ev[c % 2]: e.tensor_tensor(
                d.ap(), s, s, ALU.mult),
                waits=([sq_busy[c % 2]] if sq_busy[c % 2] else []) + w0)
            t_q = mk.mm(C.PS7[32:33, :], C.ONEC[:, 0:1], ev[c % 2].ap(),
                        start=(c == 0), stop=(c == NCD - 1),
                        waits=[t_sq], pub=True, tp=(0, 32))
            sq_busy[c % 2] = t_q
        t_m = mk.vop(lambda e: e.tensor_scalar(
            rowm, C.PS6[32:33, :], 1.0 / D, None, ALU.mult),
            waits=[t_s] + prev_bc)
        t_m2 = mk.vop(lambda e: e.tensor_tensor(rowv, rowm, rowm, ALU.mult),
                      waits=[t_m])
        t_v = mk.vop(lambda e: e.scalar_tensor_tensor(
            rowv, C.PS7[32:33, :], 1.0 / D, rowv, ALU.mult, ALU.subtract),
            waits=[t_q, t_m2])
        t_sd = mk.act(rowv, rowv, AF.Sqrt, bias=LN_EPS, waits=[t_v])
        t_r = mk.vop(lambda e: e.reciprocal(rowv, rowv), waits=[t_sd])
        t_bm = mk.mm(C.PS6[:, :], C.ONER[32:33, :], rowm, start=True,
                     stop=True, waits=[t_m2], pub=True)
        t_br = mk.mm(C.PS7[:, :], C.ONER[32:33, :], rowv, start=True,
                     stop=True, waits=[t_r], pub=True)
        prev_bc = [t_bm, t_br]
        prev_apply = []
        for c in range(NCD):
            src = src_chunk(c, tb)
            dst = dst_chunk(c, tb)
            t1 = mk.vop(lambda e, s=src: e.tensor_tensor(
                evo.ap(), s, C.PS6[:, :], ALU.subtract),
                waits=[t_bm] + list(waits))
            if lnwb:
                t2 = mk.vop(lambda e: e.tensor_tensor(
                    evo.ap(), evo.ap(), C.PS7[:, :], ALU.mult),
                    waits=[t_br])
                t3 = mk.vop(lambda e, d=dst, cc=c: e.tensor_scalar(
                    d, evo.ap(), C.BIASC[:, 8 + cc:9 + cc],
                    C.BIASC[:, 16 + cc:17 + cc], ALU.mult, ALU.add),
                    waits=[t2])
            else:
                t3 = mk.vop(lambda e, d=dst: e.tensor_tensor(
                    d, evo.ap(), C.PS7[:, :], ALU.mult), waits=[t_br])
            prev_apply.append(t3)
            out_toks.append(t3)
    return out_toks


def matseq(C, wT_ap3, src_chunk, mcs, evac, waits, ws, wq="gpsimd",
           wt_bf16=False):
    mk = C.mk
    nkc = wT_ap3.shape[0]
    wtoks = {}
    cons = {0: [], 1: []}

    def wslab(mc, buf):
        return mk.dma(wq, ws[:, nkc * 128 * buf:nkc * 128 * (buf + 1)],
                      wT_ap3[:, :, 128 * mc:128 * (mc + 1)].transpose([1, 0, 2]),
                      waits=cons[buf] + list(waits))

    wtoks[0] = wslab(0, 0)
    if mcs > 1:
        wtoks[1] = wslab(1, 1)
    qfree = [()] * 4
    toks = []
    t_mm = None
    for mc in range(mcs):
        buf = mc % 2
        for tb in range(NTB):
            q = (mc * NTB + tb) % 4
            ps = C.PSB[:, 512 * q:512 * (q + 1)]
            for kc in range(nkc):
                w = [wtoks[mc]] + (list(waits) if kc == 0 else [])
                if kc == 0:
                    w += list(qfree[q])
                t_mm = mk.mm(ps, ws[:, nkc * 128 * buf + 128 * kc:
                                    nkc * 128 * buf + 128 * (kc + 1)],
                             src_chunk(kc, tb), start=(kc == 0),
                             stop=(kc == nkc - 1), waits=w,
                             pub=(kc == nkc - 1))
            t_e = evac(mc, tb, ps, t_mm)
            qfree[q] = (t_e,)
            toks.append(t_e)
        if mc + 2 < mcs:
            cons[buf] = [t_mm]
            wtoks[mc + 2] = wslab(mc + 2, buf)
    return toks


def build_forward(C):
    mk = C.mk

    xn = C.S0
    C.ln1_toks = ln_feat(C, lambda c, tb: XSc(C, c, tb),
                         lambda c, tb: slot_c(xn, c, tb), [C.tok_x])
    tap(C, "xn1", xn.ap(), C.ln1_toks)
    if STOP_AFTER == "ln1":
        C.tok_stream = [C.tok_x]
        return

    qkv_l0(C, xn)
    tap(C, "q0f", C.S1.ap(), C.tok_cc1_ready)
    tap(C, "cc1", C.cc1_in.ap(), C.tok_cc1_ready)
    if STOP_AFTER == "qkv0":
        C.tok_stream = [C.tok_x] + C.tok_cc1_ready
        return

    t_pair = gather_kv(C, C.tok_cc1_ready, layer=0)
    t_negc = l0_bound(C, t_pair)

    tap(C, "scol", C.SM[:, 3616:3632], t_negc)
    C.post_att_tap = True
    tap(C, "qnsm", C.SM[0:16, 0:2048], [C.tok_qn])
    tap(C, "p1k", C.pair1[0:1024 * 2048].rearrange("(p t) -> p t", p=1024),
        t_pair)
    tap(C, "ccok", C.cc1_out[0:1024 * 2048].rearrange("(p t) -> p t", p=1024),
        t_pair)
    y = C.S0
    att_toks = attention(C, layer=0, QF=C.S1, Y=y, t_pair=t_pair,
                         t_extra=t_negc)
    tap(C, "y0", y.ap(), att_toks)
    tap(C, "dbgP", C.SM[0:128, 2592:3616], [C.dbg_tok] + att_toks[0:1])
    if STOP_AFTER == "att0":
        C.tok_stream = [C.tok_x] + att_toks
        return

    projT3 = C.projT.ap().rearrange("(c p) m -> c p m", p=128)

    def evac_p0(mc, tb, ps, t_mm):
        return mk.vop(lambda e, m=mc, t=tb, p=ps: e.scalar_tensor_tensor(
            XSc(C, m, t), p, 0.0, XSc(C, m, t), ALU.add, ALU.add),
            waits=[t_mm, C.tok_x])

    ws_p0 = C.SCR.sub(0, 2048)
    C.tok_stream = matseq(C, projT3, lambda kc, tb: slot_c(y, kc, tb),
                          NCD, evac_p0, att_toks, ws_p0)
    tap(C, "x1", C.XS[:, :], C.tok_stream)
    if STOP_AFTER == "x1":
        return

    if STOP_AFTER == "ffn0nt":
        C.tok_tern = [[], []]
        C.tok_tern_sc = []
        C.tok_tern_done = []
        ffn0(C)
        tap(C, "x2", C.XS[:, :], C.tok_stream)
        return
    ternary(C, att_toks)
    if STOP_AFTER == "tern":
        C.tok_stream = C.tok_stream + [C.tok_tern[0][-1], C.tok_tern[1][-1]]
        return

    ffn0(C)
    tap(C, "x2", C.XS[:, :], C.tok_stream)
    if STOP_AFTER in ("x2", "ffn0a", "ffn0b", "ffn0c", "ffn0d", "tern",
                      "ffn0t", "ffn0s", "ffn0rs", "ffn0e1", "ffn0e2", "ffn0e3"):
        return

    att_block1(C)
    tap(C, "x3", C.XS[:, :], C.tok_stream)
    if STOP_AFTER == "x3":
        return

    ffn1(C)


def qkv_l0(C, xn):
    mk = C.mk
    QF = C.S1
    ws1 = C.SCR.sub(0, 2048)
    kst = C.SCR.sub(2048, 2048)
    vst = C.SCR.sub(4096, 2080)
    evf = [C.SCRF.sub(3072, 512), C.SCRF.sub(3584, 512)]
    t_z1 = mk.vop(lambda e: e.memset(sm(C, QN), 0.0))
    t_z2 = mk.vop(lambda e: e.memset(sm(C, KN), 0.0))
    qkvT3 = C.qkvT.ap().rearrange("(c p) m -> c p m", p=128)
    wtoks = {}
    cons = {0: [], 1: []}

    def wslab(mc, buf):
        return mk.dma("gpsimd", ws1[:, 1024 * buf:1024 * (buf + 1)],
                      qkvT3[:, :, 128 * mc:128 * (mc + 1)].transpose([1, 0, 2]),
                      waits=cons[buf] + C.tok_consts)

    wtoks[0] = wslab(0, 0)
    wtoks[1] = wslab(1, 1)
    qfree = [()] * 4
    sq_busy = [None, None]
    acc = {QN: t_z1, KN: t_z2}
    kst_free = []
    stage_toks = []
    t_mm = None
    for mc in range(16):
        buf = mc % 2
        for tb in range(NTB):
            q = (mc * NTB + tb) % 4
            ps = C.PSB[:, 512 * q:512 * (q + 1)]
            for kc in range(NCD):
                w = [wtoks[mc], C.ln1_toks[tb * NCD + kc]]
                if kc == 0:
                    w += list(qfree[q])
                t_mm = mk.mm(ps, ws1[:, 1024 * buf + 128 * kc:
                                     1024 * buf + 128 * (kc + 1)],
                             slot_c(xn, kc, tb), start=(kc == 0),
                             stop=(kc == NCD - 1), waits=w,
                             pub=(kc == NCD - 1))
            if mc < 8:
                t_e = mk.act(slot_c(QF, mc, tb), ps, AF.Copy, waits=[t_mm])
            else:
                t_e = mk.act(kst[:, 512 * tb:512 * (tb + 1)], ps, AF.Copy,
                             waits=[t_mm] + (kst_free if tb == 0 else []))
            reg = QN if mc < 8 else KN
            c = mc % 8
            rb = reg[0]
            t_sq = mk.act(evf[buf].ap(), ps, AF.Square,
                          waits=[t_mm] + ([sq_busy[buf]] if sq_busy[buf] else []))
            t_pm = mk.mm(C.PS4[rb:rb + 16, :], C.HPAT[:, 16 * c:16 * (c + 1)],
                         evf[buf].ap(), start=True, stop=True,
                         waits=[t_sq, acc[reg]], pub=True, tp=(0, rb))
            sq_busy[buf] = t_pm
            t_ac = mk.vop(lambda e, r=reg, t=tb: e.tensor_tensor(
                C.SM[r[0]:r[1], 512 * t:512 * (t + 1)],
                C.SM[r[0]:r[1], 512 * t:512 * (t + 1)],
                C.PS4[r[0]:r[0] + 16, :], ALU.add), waits=[t_pm, acc[reg]])
            acc[reg] = t_ac
            qfree[q] = (t_e, t_sq)
        if mc >= 8:
            koff = (mc - 8) * 128 * 2048
            t_d = mk.dma("sync",
                         C.cc1_in[koff:koff + 128 * 2048].rearrange(
                             "(p t) -> p t", p=128),
                         kst.ap(), waits=[t_e])
            kst_free = [t_d]
            stage_toks.append(t_d)
        if mc + 2 < 16:
            cons[buf] = [t_mm]
            wtoks[mc + 2] = wslab(mc + 2, buf)
    C.tok_qn = acc[QN]
    t_kn_done = acc[KN]

    ws2 = C.S2.sub(0, 8192)
    vw = qkvT3[:, :, 2048:3072]
    t_w = []
    for mb in range(2):
        t_w.append(mk.dma(
            "gpsimd", ws2[:, 4096 * mb:4096 * (mb + 1)],
            vw[:, :, 512 * mb:512 * (mb + 1)].transpose([1, 0, 2]),
            waits=C.tok_consts))
    ones_ap = vst.ap().rearrange("p (b g pr w) -> p b g pr w", g=4, pr=2,
                                 w=65)[:, :, :, :, 64:65]
    t_ones = mk.vop(lambda e: e.memset(ones_ap, 1.0), waits=[C.tok_x])
    vst_free = [[t_ones], [t_ones]]
    for tc in range(NTC):
        buf = tc % 2
        etoks = []
        for mb in range(2):
            q = (tc * 2 + mb) % 4
            ps = C.PSB[:, 512 * q:512 * (q + 1)]
            for kc in range(NCD):
                w = [t_w[mb]]
                if kc == 0:
                    w += list(qfree[q])
                lhsT = xn[:, 2048 * kc + 128 * tc:2048 * kc + 128 * (tc + 1)]
                t_mm = mk.mm(ps, lhsT,
                             ws2[:, 4096 * mb + 512 * kc:4096 * mb + 512 * (kc + 1)],
                             start=(kc == 0), stop=(kc == NCD - 1), waits=w,
                             pub=(kc == NCD - 1))
            dst4 = vst[:, 1040 * buf:1040 * (buf + 1)].rearrange(
                "p (hp pr w) -> p hp pr w", pr=2, w=65)[:, 4 * mb:4 * (mb + 1),
                                                        :, 0:64]
            src4 = ps.rearrange("p (hp pr d) -> p hp pr d", pr=2, d=64)
            t_e = mk.vop(lambda e, d=dst4, s=src4: e.tensor_copy(d, s),
                         waits=[t_mm] + (vst_free[buf] if mb == 0 else []))
            etoks.append(t_e)
            qfree[q] = (t_e,)
        voff = KBE + tc * 128 * 1040
        t_d = mk.dma("sync",
                     C.cc1_in[voff:voff + 128 * 1040].rearrange(
                         "(p t) -> p t", p=128),
                     vst[:, 1040 * buf:1040 * (buf + 1)], waits=etoks)
        vst_free[buf] = [t_d]
        stage_toks.append(t_d)
    t_kn = mk.dma("sync",
                  C.cc1_in[KBE + VBE:].bitcast(F32).rearrange(
                      "(p t) -> p t", p=16),
                  sm(C, KN), waits=[t_kn_done])
    stage_toks.append(t_kn)
    C.tok_cc1_ready = stage_toks + [t for t_ in qfree for t in t_]




def gather_kv(C, ready, layer):
    mk = C.mk
    t_ag = mk.ccop("AllGather", ALU.bypass, [C.cc1_in.ap().opt()],
                   [C.cc1_out.ap().opt()], waits=ready)

    def cp(e):
        pid = C.nc.partition_id(engines=[mybir.EngineType.Pool])
        seg = C.cc1_out.ap().rearrange("(r n) -> r n", n=2 * CC1)
        return e.dma_start(out=C.pair1[:].rearrange("(o n) -> o n", o=1),
                           in_=seg[bass.ts(pid // 2, 1), :])

    t_cp = mk.op("gpsimd", cp, waits=[t_ag], inc="dgp", inc_by=16)
    return [t_cp]


def l0_bound(C, t_pair):
    mk = C.mk
    st = sm(C, ST16)
    kna = sm(C, KNATMP)
    t_run = None
    for r in range(2):
        off = r * CC1 + KBE + VBE
        t_l = mk.dma("sync", kna,
                     C.pair1[off:off + NBE].bitcast(F32).rearrange(
                         "(p t) -> p t", p=16),
                     waits=list(t_pair) + ([t_run] if t_run else []))
        t_mx = mk.vop(lambda e, rr=r: e.reduce_max(st[:, 4 + rr:5 + rr], kna,
                                                   axis=AX.X), waits=[t_l])
        t_run = t_mx
    t3 = mk.vop(lambda e: e.tensor_tensor(st[:, 4:5], st[:, 4:5], st[:, 5:6],
                                          ALU.max), waits=[t_run], strict=True)
    t4 = mk.vop(lambda e: e.reduce_max(st[:, 6:7], sm(C, QN), axis=AX.X),
                waits=[C.tok_qn])
    t5 = mk.act(st[:, 4:5], st[:, 4:5], AF.Sqrt, waits=[t3])
    t6 = mk.act(st[:, 6:7], st[:, 6:7], AF.Sqrt, waits=[t4])
    t7 = mk.vop(lambda e: e.tensor_tensor(st[:, 0:1], st[:, 4:5], st[:, 6:7],
                                          ALU.mult), waits=[t5, t6])
    t8 = mk.vop(lambda e: e.tensor_scalar(st[:, 0:1], st[:, 0:1], -0.125,
                                          None, ALU.mult), waits=[t7],
                strict=True)
    t10 = mk.transp(C.PS4[0:16, 0:16], st[:, 0:16], C.IDF[0:16, 0:16],
                    waits=[t8], pub=True)
    rowp = C.SM[0:1, 2080:2088]
    t11 = mk.vop(lambda e: e.tensor_reduce(
        rowp.unsqueeze(2), C.PS4[0:1, 0:16].rearrange("o (hp pr) -> o hp pr",
                                                      pr=2),
        AX.X, ALU.min), waits=[t10])
    t12 = mk.mm(C.PS4[:, 16:24], C.ONER[0:1, :], rowp, start=True, stop=True,
                waits=[t11], pub=True)
    t13 = mk.vop(lambda e: e.tensor_copy(sm(C, SCOL)[:, 8:16], C.PS4[:, 16:24]),
                 waits=[t12])
    return [t13]


def attention(C, layer, QF, Y, t_pair, t_extra):
    mk = C.mk
    ksl = C.S2.sub(0, 8192)
    vsl = C.S2.sub(8192, 8384)
    pbuf = C.SCR.sub(0, 4096)
    mbuf = C.SCR.sub(4096, 4096)
    gsets = [_gset(i) if layer == 0 else list(range(8)) for i in range(4)]
    out_toks = []
    av_free = {0: (), 1: ()}
    ps6_cons = None
    ps7_cons = None
    psb_free = None
    pb_cons = [None, None]
    mtoks = [None] * 4
    kvtoks = {}
    buf_cons = {0: list(t_pair), 1: list(t_pair)}

    def loadkv(i, hp, buf, waits):
        toks = []
        nG = len(gsets[i]) // 2
        for r in range(2):
            off = r * CC1
            kview = C.pair1[off:off + KBE].rearrange("(p t) -> p t", p=1024)
            toks.append(mk.dma(
                "sync",
                ksl[:, 4096 * buf + 2048 * r:4096 * buf + 2048 * r + 512 * nG],
                kview[128 * hp:128 * (hp + 1), 0:512 * nG], waits=waits))
            voff = off + KBE
            vview = C.pair1[voff:voff + VBE].rearrange(
                "(s p t) -> s p t", p=128, t=1040)
            toks.append(mk.dma(
                "scalar",
                vsl[:, 4160 * buf + 2080 * r:
                    4160 * buf + 2080 * r + 520 * nG].rearrange(
                        "p (s w) -> p s w", w=130),
                vview[0:4 * nG, :, 130 * hp:130 * (hp + 1)].transpose([1, 0, 2]),
                waits=waits))
        return toks

    seq = [(i, hp) for i in range(4) for hp in range(8)]
    kvtoks[seq[0]] = loadkv(0, 0, 0, list(t_pair))
    kvtoks[seq[1]] = loadkv(0, 1, 1, list(t_pair))
    nxt = 2
    for si, (i, hp) in enumerate(seq):
        buf = si % 2
        gs = gsets[i]
        if layer == 0 and hp == 0:
            mtoks[i] = mk.dma(
                "sync", mbuf.ap().rearrange("p (j q) -> p j q", q=512),
                C.masks_in.ap()[i].transpose([1, 0, 2]),
                waits=(out_toks[-1:] if i > 0 else list(t_pair)))
        first_av = True
        t_av4 = t_av5 = t_dn = None
        bi = 0
        nch = 4 * len(gs)
        chi = 0
        for gi, G in enumerate(gs):
            r, gloc = G // 4, G % 4
            kbase = 4096 * buf + 2048 * r + 512 * gloc
            vbase = 4160 * buf + 2080 * r + 520 * gloc
            for cchalf in range(2):
                t_mm = None
                for u in range(2):
                    cc = 2 * cchalf + u
                    for pr in range(2):
                        lhsT = ksl[64 * pr:64 * pr + 64,
                                   kbase + 128 * cc:kbase + 128 * (cc + 1)]
                        rhs = slot_c(QF, hp, i)[64 * pr:64 * pr + 64, :]
                        w = list(kvtoks[(i, hp)]) + list(t_extra)
                        if u == 0 and pr == 0 and psb_free is not None:
                            w.append(psb_free)
                        t_mm = mk.mm(C.PSB[:, 1024 * u + 512 * pr:
                                           1024 * u + 512 * (pr + 1)],
                                     lhsT, rhs, start=True, stop=True,
                                     waits=w, pub=(u == 1 and pr == 1))
                slot = bi % 2
                pslot = pbuf[:, 2048 * slot:2048 * (slot + 1)]
                aw = [t_mm] + ([pb_cons[slot]] if pb_cons[slot] else [])
                if layer == 0:
                    t_a = mk.act(pslot, C.PSB[:, :], AF.Exp,
                                 bias=sm(C, SCOL)[:, 8 + hp:9 + hp],
                                 scale=0.125, waits=aw)
                else:
                    t_a = mk.act(pslot, C.PSB[:, :], AF.Sigmoid,
                                 bias=-THRESH * SHARP, scale=SHARP, waits=aw)
                psb_free = t_a
                if layer == 0 and G in (i, 4 + i):
                    which = 0 if G == i else 1
                    m3 = mbuf[:, 2048 * which + 1024 * cchalf:
                              2048 * which + 1024 * (cchalf + 1)].rearrange(
                        "p (cc q) -> p cc q", q=512)
                    ps4v = pslot.rearrange("p (cc pr q) -> p cc pr q",
                                           cc=2, pr=2)
                    m4 = m3.unsqueeze(2).broadcast_to([128, 2, 2, 512])
                    t_a = mk.vop(lambda e, p4=ps4v, mm4=m4: e.tensor_tensor(
                        p4, p4, mm4, ALU.mult), waits=[t_a, mtoks[i]])
                    if si == 0 and layer == 0 and gi == 0 and cchalf == 0 \
                            and not hasattr(C, "dbg_tok"):
                        C.dbg_tok = mk.vop(
                            lambda e, p=pslot: e.tensor_copy(
                                C.SM[0:128, 2592:3616], p[:, 0:1024]),
                            waits=[t_a])
                last_b = (gi == len(gs) - 1 and cchalf == 1)
                for u in range(2):
                    cc = 2 * cchalf + u
                    pch = pslot[:, 1024 * u:1024 * u + 512]
                    pcho = pslot[:, 1024 * u + 512:1024 * u + 1024]
                    w0 = [t_a] + list(kvtoks[(i, hp)])
                    we = list(w0)
                    if first_av and u == 0:
                        we += list(av_free[0])
                    t_av4 = mk.mm(C.PS4[0:65, :],
                                  vsl[:, vbase + 130 * cc:
                                      vbase + 130 * cc + 65],
                                  pch, start=(first_av and u == 0),
                                  stop=(last_b and u == 1), waits=we,
                                  pub=(last_b and u == 1))
                    wo = list(w0)
                    if first_av and u == 0:
                        wo += list(av_free[1])
                    t_av5 = mk.mm(C.PS5[64:128, :],
                                  vsl[:, vbase + 130 * cc + 65:
                                      vbase + 130 * cc + 129],
                                  pcho, start=(first_av and u == 0),
                                  stop=(last_b and u == 1), waits=wo,
                                  tp=(0, 64), pub=(u == 1))
                    if layer == 0:
                        wd = list(w0)
                        if first_av and u == 0 and ps6_cons is not None:
                            wd.append(ps6_cons)
                        t_dn = mk.mm(C.PS6[0:1, :],
                                     vsl[:, vbase + 130 * cc + 129:
                                         vbase + 130 * cc + 130],
                                     pcho, start=(first_av and u == 0),
                                     stop=(last_b and u == 1), waits=wd,
                                     pub=(u == 1))
                    first_av = False
                pb_cons[slot] = t_av5 if layer else t_dn
                bi += 1
                chi += 2
        if layer == 0:
            drow_e = sm(C, DROW)
            drow_o = sm(C, DROWO)
            t_ce = mk.vop(lambda e: e.tensor_copy(drow_e, C.PS4[64:65, :]),
                          waits=[t_av4])
            t_re = mk.vop(lambda e: e.reciprocal(drow_e, drow_e),
                          waits=[t_ce])
            t_co = mk.vop(lambda e: e.tensor_copy(drow_o, C.PS6[0:1, :]),
                          waits=[t_dn])
            t_ro = mk.vop(lambda e: e.reciprocal(drow_o, drow_o),
                          waits=[t_co])
            t_be = mk.mm(C.PS6[:, :], C.ONER[64:65, :], drow_e, start=True,
                         stop=True, waits=[t_re, t_ro], pub=True)
            t_bo = mk.mm(C.PS7[:, :], C.ONER[0:1, :], drow_o, start=True,
                         stop=True,
                         waits=[t_ro] + ([ps7_cons] if ps7_cons else []),
                         pub=True)
        else:
            qrhs = slot_c(QF, hp, i)
            t_be = mk.mm(C.PS6[:, :], C.GQB[:, 0:128], qrhs, start=True,
                         stop=True,
                         waits=[t_av4] + ([ps6_cons] if ps6_cons else []),
                         pub=True)
            t_bo = mk.mm(C.PS7[:, :], C.GQB[:, 128:256], qrhs, start=True,
                         stop=True,
                         waits=[t_av5] + ([ps7_cons] if ps7_cons else []),
                         pub=True)
        yde = slot_c(Y, hp, i)[0:64, :]
        ydo = slot_c(Y, hp, i)[64:128, :]
        t_ce2 = mk.act(yde, C.PS4[0:64, :], AF.Copy, waits=[t_av4])
        t_ye = mk.vop(lambda e, d=yde: e.tensor_tensor(
            d, d, C.PS6[0:64, :], ALU.mult), waits=[t_be, t_ce2])
        t_co2 = mk.act(ydo, C.PS5[64:128, :], AF.Copy, waits=[t_av5])
        t_yo = mk.vop(lambda e, d=ydo: e.tensor_tensor(
            d, d, C.PS7[64:128, :], ALU.mult), waits=[t_bo, t_co2])
        av_free[0] = (t_ye,)
        av_free[1] = (t_yo,)
        ps6_cons = t_ye
        ps7_cons = t_yo
        out_toks += [t_ye, t_yo]
        buf_cons[buf] = [t_yo]
        if nxt < len(seq):
            (i2, hp2) = seq[nxt]
            kvtoks[(i2, hp2)] = loadkv(i2, hp2, nxt % 2, buf_cons[nxt % 2])
            nxt += 1
    C.tok_pair_readers = out_toks[-2:]
    return out_toks


def ternary(C, entry_waits):
    mk = C.mk
    gbuf = C.SCR.sub(4096, 1024)
    slab = C.SCRF.sub(2560, 1024)
    tacc = sm(C, TACC)
    tred = sm(C, TRED)
    t_z = mk.vop(lambda e: e.memset(tacc, 0.0), waits=list(entry_waits))
    C.tok_tern = []
    scol = sm(C, SCOL)
    t_sc = None
    for wi, (wt, wqd, ncol) in enumerate(((C.w1T, C.w1q_d, 4 * D),
                                          (C.w2T, C.w2q_d, D))):
        nrow = (D, 4 * D)[wi]
        wt3 = wt.ap().rearrange("(c p) m -> c p m", p=128)
        wq3 = wqd.ap().rearrange("(c p) m -> c p m", p=128)
        slabs = [(c, j) for c in range(nrow // 128) for j in range(ncol // 512)]
        tlast = [t_z] + list(entry_waits), [t_z] + list(entry_waits)
        tlast = [[t_z] + list(entry_waits), [t_z] + list(entry_waits)]
        acc_t = t_z
        for n, (c, j) in enumerate(slabs):
            b = n % 2
            sl = slab.sub(512 * b, 512)
            t_l = mk.dma("scalar", sl.ap(), wt3[c, :, 512 * j:512 * (j + 1)],
                         waits=tlast[b])
            t_r = mk.vop(lambda e, s=sl, bb=b, w_=wi: e.reduce_sum(
                tred[:, bb + 2 * w_:bb + 2 * w_ + 1], s.ap(), axis=AX.X,
                apply_absolute_value=True), waits=[t_l])
            t_a = mk.vop(lambda e, bb=b, w_=wi: e.tensor_tensor(
                tacc[:, w_:w_ + 1], tacc[:, w_:w_ + 1],
                tred[:, bb + 2 * w_:bb + 2 * w_ + 1], ALU.add),
                waits=[t_r, acc_t], strict=True)
            acc_t = t_a
            tlast[b] = [t_r]
        t_t = mk.mm(C.PS7[0:1, 0:1], C.ONEC[:, 0:1], tacc[:, wi:wi + 1],
                    start=True, stop=True, waits=[acc_t], pub=True)
        row3 = C.SM[0:1, 2088 + 8 * wi:2091 + 8 * wi]
        nelem = float(nrow * ncol)
        if wi == 1:
            C.dbgT = mk.vop(lambda e: e.tensor_copy(
                C.SM[0:1, 2100:2101], C.PS7[0:1, 0:1]), waits=[t_t])
            C.dbgU = mk.vop(lambda e: e.tensor_copy(
                C.SM[0:128, 3664:3666], tacc[:, 0:2]), waits=[t_t])
        t_mn = mk.vop(lambda e, r3=row3, ne=nelem: e.tensor_scalar(
            r3[0:1, 0:1], C.PS7[0:1, 0:1], 1.0 / ne, 1e-8,
            ALU.mult, ALU.max), waits=[t_t])
        t_r4 = mk.vop(lambda e, r3=row3: e.tensor_scalar(
            r3[0:1, 1:2], r3[0:1, 0:1], 0.5, None, ALU.mult),
            waits=[t_mn], strict=True)
        t_r5 = mk.vop(lambda e, r3=row3: e.tensor_scalar(
            r3[0:1, 2:3], r3[0:1, 0:1], -0.5, None, ALU.mult),
            waits=[t_mn], strict=True)
        t_b = mk.mm(C.PS7[:, 0:3], C.ONER[0:1, :], row3, start=True,
                    stop=True, waits=[t_r4, t_r5], pub=True)
        t_sc = mk.vop(lambda e, w_=wi: e.tensor_copy(
            scol[:, 3 * w_:3 * w_ + 3], C.PS7[:, 0:3]), waits=[t_b])
        tlast2 = [[t_sc], [t_sc]]
        q_toks = []
        for n, (c, j) in enumerate(slabs):
            b = n % 2
            sl = slab.sub(512 * b, 512)
            g1 = gbuf.sub(512 * b, 512)
            t_l = mk.dma("scalar", sl.ap(), wt3[c, :, 512 * j:512 * (j + 1)],
                         waits=tlast2[b])
            t_g1 = mk.vop(lambda e, s=sl, g=g1, w_=wi: e.tensor_scalar(
                g.ap(), s.ap(), scol[:, 3 * w_ + 1:3 * w_ + 2], None,
                ALU.is_gt), waits=[t_l])
            t_b2 = mk.vop(lambda e, s=sl, w_=wi: e.tensor_scalar(
                s.ap(), s.ap(), scol[:, 3 * w_ + 2:3 * w_ + 3], None,
                ALU.is_lt), waits=[t_g1])
            t_q = mk.vop(lambda e, s=sl, g=g1: e.tensor_tensor(
                g.ap(), g.ap(), s.ap(), ALU.subtract), waits=[t_b2])
            t_d = mk.dma("scalar", wq3[c, :, 512 * j:512 * (j + 1)],
                         g1.ap(), waits=[t_q])
            tlast2[b] = [t_d]
            q_toks.append(t_d)
        C.tok_tern.append(q_toks)
    C.tok_tern_sc = [t_sc]
    C.tok_tern_done = [C.tok_tern[0][-1], C.tok_tern[1][-1], t_sc]
    tap(C, "terntap", C.SM[0:128, 3632:3666],
        [C.dbgT, C.dbgU, t_sc] + C.tok_tern_done)


def ffn0(C):
    mk = C.mk
    xtk = C.S2
    E = C.E64
    EBW = 256
    NEB = 2048 // EBW

    ev = C.SCRF.sub(0, 1024)
    ev2 = C.SCRF.sub(1024, 1024)
    lns = sm(C, LNS)
    ln_toks = []
    half_free = [None, None]
    stat_busy = None
    for tc in range(NTC):
        half = tc % 2
        pbase = 1024 * half
        t_t = None
        for c in range(NCD):
            t_t = mk.transp(
                C.PSB[:, pbase + 128 * c:pbase + 128 * (c + 1)],
                XSc(C, c)[:, 128 * tc:128 * (tc + 1)], C.IDF[:, :],
                waits=C.tok_stream +
                ([half_free[half]] if half_free[half] else []),
                pub=(c == NCD - 1))
        ps = C.PSB[:, pbase:pbase + 1024]
        w0 = [t_t] + ([stat_busy] if stat_busy else [])
        t_cp = mk.vop(lambda e, p=ps: e.tensor_copy(ev.ap(), p), waits=w0)
        t_s = mk.vop(lambda e: e.reduce_sum(lns[:, 0:1], ev.ap(), axis=AX.X),
                     waits=[t_cp])
        t_sq = mk.vop(lambda e: e.tensor_tensor(ev2.ap(), ev.ap(), ev.ap(),
                                                ALU.mult), waits=[t_cp])
        t_q = mk.vop(lambda e: e.reduce_sum(lns[:, 1:2], ev2.ap(), axis=AX.X),
                     waits=[t_sq])
        t_m = mk.vop(lambda e: e.tensor_scalar(
            lns[:, 2:3], lns[:, 0:1], 1.0 / D, None, ALU.mult),
            waits=[t_s], strict=True)
        t_m2 = mk.vop(lambda e: e.tensor_tensor(
            lns[:, 3:4], lns[:, 2:3], lns[:, 2:3], ALU.mult),
            waits=[t_m], strict=True)
        t_v = mk.vop(lambda e: e.scalar_tensor_tensor(
            lns[:, 1:2], lns[:, 1:2], 1.0 / D, lns[:, 3:4],
            ALU.mult, ALU.subtract), waits=[t_q, t_m2], strict=True)
        t_sd = mk.act(lns[:, 1:2], lns[:, 1:2], AF.Sqrt, bias=LN_EPS,
                      waits=[t_v])
        t_r = mk.vop(lambda e: e.reciprocal(lns[:, 1:2], lns[:, 1:2]),
                     waits=[t_sd])
        t_nm = mk.vop(lambda e: e.tensor_tensor(
            lns[:, 2:3], lns[:, 2:3], lns[:, 1:2], ALU.mult),
            waits=[t_r, t_m], strict=True)
        t_x = mk.vop(lambda e, t=tc: e.tensor_scalar(
            xtk[:, 1024 * t:1024 * (t + 1)], ev.ap(),
            lns[:, 1:2], lns[:, 2:3], ALU.mult, ALU.subtract),
            waits=[t_nm], strict=True)
        half_free[half] = t_cp
        stat_busy = t_x
        ln_toks.append(t_x)
    tap(C, "xn2tok", xtk.sub(0, 16384).ap(), ln_toks)
    if STOP_AFTER in ("ffn0a", "ffn0t", "ffn0s"):
        C.tok_stream = C.tok_stream + ln_toks
        return

    pslab = C.SCR.sub(2048, 2048)
    evb = C.SCRF.sub(0, 256)
    phi3 = C.phiT.ap().rearrange("(c p) m -> c p m", p=128)
    scol = sm(C, SCOL)
    t_mb = mk.vop(lambda e: e.reduce_max(C.SM[0:1, 3968:3969],
                                         C.PHIBS[0:1, :], axis=AX.X),
                  waits=C.tok_consts)
    t_nz = mk.vop(lambda e: e.memset(C.SM[0:1, 3969:3970], 0.0),
                  waits=[ln_toks[-1]])
    norm_t = t_nz
    slab_cons = list(C.tok_stream) + [ln_toks[-1]]
    for eb in range(NEB):
        t_l = mk.dma("gpsimd", pslab.ap().rearrange("p (c m) -> p c m", m=EBW),
                     phi3[:, :, EBW * eb:EBW * (eb + 1)].transpose([1, 0, 2]),
                     waits=slab_cons + C.tok_consts)
        t_pm = None
        t_sq = None
        for kc in range(NCD):
            t_sq = mk.vop(lambda e, k=kc: e.tensor_tensor(
                evb.ap(), pslab[:, EBW * k:EBW * (k + 1)],
                pslab[:, EBW * k:EBW * (k + 1)], ALU.mult),
                waits=[t_l, norm_t] + ([t_pm] if t_pm else []))
            t_pm = mk.mm(C.PS5[0:1, 0:EBW], C.ONEC[:, 0:1], evb.ap(),
                         start=(kc == 0), stop=(kc == NCD - 1),
                         waits=[t_sq], pub=True)
        t_mx = mk.vop(lambda e: e.reduce_max(C.SM[0:1, 3970:3971],
                                             C.PS5[0:1, 0:EBW], axis=AX.X),
                      waits=[t_pm])
        norm_t = mk.vop(lambda e: e.tensor_tensor(
            C.SM[0:1, 3969:3970], C.SM[0:1, 3969:3970],
            C.SM[0:1, 3970:3971], ALU.max), waits=[t_mx], strict=True)
        slab_cons = [t_sq]
    t_s1 = mk.act(C.SM[0:1, 3969:3970], C.SM[0:1, 3969:3970], AF.Sqrt,
                  waits=[norm_t])
    t_s2 = mk.vop(lambda e: e.tensor_scalar(
        C.SM[0:1, 3969:3970], C.SM[0:1, 3969:3970], 32.0, None, ALU.mult),
        waits=[t_s1])
    t_s3 = mk.vop(lambda e: e.tensor_tensor(
        C.SM[0:1, 3969:3970], C.SM[0:1, 3969:3970], C.SM[0:1, 3968:3969],
        ALU.add), waits=[t_s2, t_mb], strict=True)
    t_s4 = mk.vop(lambda e: e.tensor_scalar(
        C.SM[0:1, 3969:3970], C.SM[0:1, 3969:3970], -1.0, None, ALU.mult),
        waits=[t_s3], strict=True)
    t_s5 = mk.mm(C.PS5[:, 0:1], C.ONER[0:1, :], C.SM[0:1, 3969:3970],
                 start=True, stop=True, waits=[t_s4], pub=True)
    t_cl = mk.vop(lambda e: e.tensor_copy(scol[:, 6:7], C.PS5[:, 0:1]),
                  waits=[t_s5])
    norm_done = [t_cl]
    tap(C, "cltap", C.SM[0:128, 3616:3632], norm_done)
    tap(C, "clrow", C.SM[0:1, 3966:3972], [t_s4, t_mb])

    ltile = C.SCR.sub(0, 2048)
    e_toks = {}
    lt_cons = {0: norm_done + [ln_toks[-1]], 1: norm_done + [ln_toks[-1]]}
    slab_cons2 = slab_cons
    psb_free = [None, None]
    t_mm = None
    for eb in range(NEB):
        t_l = mk.dma("gpsimd", pslab.ap().rearrange("p (c m) -> p c m", m=EBW),
                     phi3[:, :, EBW * eb:EBW * (eb + 1)].transpose([1, 0, 2]),
                     waits=slab_cons2)
        for tc in range(NTC):
            lb = tc % 2
            t_t = None
            for kc in range(NCD):
                t_t = mk.transp(
                    C.PS4B[:, 128 * kc:128 * (kc + 1)],
                    xtk[:, 1024 * tc + 128 * kc:1024 * tc + 128 * (kc + 1)],
                    C.IDB[:, :],
                    waits=[ln_toks[tc]] + (lt_cons[lb] if kc == 0 else []),
                    pub=(kc == NCD - 1))
            t_lt = mk.vop(lambda e, b=lb: e.tensor_copy(
                ltile[:, 1024 * b:1024 * (b + 1)], C.PS4B[:, 0:1024]),
                waits=[t_t])
            half = tc % 2
            ps = C.PSB[:, 1024 * half:1024 * half + EBW]
            for kc in range(NCD):
                w = [t_lt, t_l]
                if kc == 0 and psb_free[half]:
                    w.append(psb_free[half])
                t_mm = mk.mm(ps, ltile[:, 1024 * lb + 128 * kc:
                                       1024 * lb + 128 * (kc + 1)],
                             pslab[:, EBW * kc:EBW * (kc + 1)],
                             start=(kc == 0), stop=False, waits=w)
            t_bias = mk.mm(ps, C.ONERB[0:1, :],
                           C.PHIBS[0:1, EBW * eb:EBW * (eb + 1)],
                           start=False, stop=True, waits=[], pub=True)
            t_e = mk.act(E[:, 2048 * tc + EBW * eb:2048 * tc + EBW * (eb + 1)],
                         ps, AF.Exp, bias=scol[:, 6:7], scale=1.0,
                         waits=[t_bias] + norm_done)
            e_toks[(tc, eb)] = t_e
            psb_free[half] = t_e
            lt_cons[lb] = [t_bias]
        slab_cons2 = [t_bias]
    e_last = e_toks[(NTC - 1, NEB - 1)]
    t_ps4_free = [e_last]
    tap(C, "Etap", C.E64.ap(), [e_last])
    if STOP_AFTER == "ffn0b":
        C.tok_stream = C.tok_stream + [e_last]
        return

    rd_toks = {}
    rc_toks = {}
    for tc in range(NTC):
        ew = [e_toks[(tc, eb)] for eb in range(NEB)]
        erow = E[:, 2048 * tc:2048 * (tc + 1)]
        t_rd = mk.vop(lambda e, s=erow, t=tc: e.reduce_sum(
            sm(C, RD_)[:, 64 * t:64 * (t + 1)].unsqueeze(2),
            s.rearrange("p (n q) -> p n q", q=32), axis=AX.X), waits=ew)
        t_rdi = mk.vop(lambda e, t=tc: e.reciprocal(
            sm(C, RD_)[:, 64 * t:64 * (t + 1)],
            sm(C, RD_)[:, 64 * t:64 * (t + 1)]), waits=[t_rd], strict=True)
        rd_toks[tc] = t_rdi
        t_rc = mk.vop(lambda e, s=erow, t=tc: e.reduce_sum(
            sm(C, RC_)[:, 32 * t:32 * (t + 1)].unsqueeze(2),
            s.rearrange("p (n q) -> p q n", q=32), axis=AX.X), waits=ew)
        t_rci = mk.vop(lambda e, t=tc: e.reciprocal(
            sm(C, RC_)[:, 32 * t:32 * (t + 1)],
            sm(C, RC_)[:, 32 * t:32 * (t + 1)]), waits=[t_rc], strict=True)
        rc_toks[tc] = t_rci
    tap(C, "rctap", C.SM[0:128, 1024:1536], list(rc_toks.values()))
    tap(C, "rdtap", C.SM[0:128, 0:1024], list(rd_toks.values()))

    strip = C.SCR.sub(0, 2048)
    evd = C.SCRF.sub(1024, 1024)
    d_toks = []
    strip_cons = [e_last]
    ps_cons = {0: None, 1: None}
    evd_cons = [None, None]
    last_disp_mm = None
    for np_ in range(16):
        s_toks = []
        for tc in range(NTC):
            e3 = E[:, 2048 * tc + 128 * np_:2048 * tc + 128 * (np_ + 1)]
            rdb = sm(C, RD_)[:, 64 * tc + 4 * np_:64 * tc + 4 * (np_ + 1)]
            t_st = mk.vop(lambda e, s=e3, rb=rdb, t=tc: e.tensor_tensor(
                strip[:, 128 * t:128 * (t + 1)].rearrange(
                    "p (n q) -> p n q", q=32),
                s.rearrange("p (n q) -> p n q", q=32),
                rb.unsqueeze(2).broadcast_to([128, 4, 32]), ALU.mult),
                waits=[rd_toks[tc]] + strip_cons)
            s_toks.append(t_st)
        strip_cons = []
        for db in range(2):
            pq = np_ % 2
            ps = (C.PS6, C.PS7)[pq]
            t_mm = None
            for tc in range(NTC):
                w = [s_toks[tc]]
                if tc == 0 and ps_cons[pq]:
                    w.append(ps_cons[pq])
                if tc == 0 and pq == 1 and ps_cons[pq] is None:
                    w.extend(C.tok_tern_sc)
                t_mm = mk.mm(ps[:, :], strip[:, 128 * tc:128 * (tc + 1)],
                             xtk[:, 1024 * tc + 512 * db:
                                 1024 * tc + 512 * (db + 1)],
                             start=(tc == 0), stop=(tc == NTC - 1),
                             waits=w, pub=(tc == NTC - 1))
            last_disp_mm = t_mm
            eb_ = evd.sub(512 * db, 512)
            t_ev = mk.vop(lambda e, p=ps, d=eb_: e.tensor_copy(d.ap(), p[:, :]),
                          waits=[t_mm] +
                          ([evd_cons[db]] if evd_cons[db] else []))
            ps_cons[pq] = t_ev
            if db == 1:
                strip_cons = [t_mm]
            t_dd = mk.dma("sync",
                          C.rs_in[128 * np_:128 * (np_ + 1),
                                  512 * db:512 * (db + 1)],
                          eb_.ap(), waits=[t_ev])
            evd_cons[db] = t_dd
            d_toks.append(t_dd)

    tap(C, "disp", C.rs_in.ap(), d_toks)
    if STOP_AFTER == "ffn0c":
        C.tok_stream = C.tok_stream + d_toks
        return
    t_rs = mk.ccop("ReduceScatter", ALU.add, [C.rs_in.ap().opt()],
                   [C.rs_out.ap().opt()], waits=d_toks)
    tap(C, "rsout", C.rs_out.ap(), [t_rs])
    if STOP_AFTER == "ffn0rs":
        C.tok_stream = C.tok_stream + [t_rs]
        return

    xred = [C.SCRF.sub(0, 1024), C.SCRF.sub(1024, 1024)]
    sqb = Buf(C.S2F.h, C.S2F.off + 7184, 1024)
    xte = C.S2.sub(8192, 2048)
    xteT = C.S2.sub(10240, 2048)
    ystmp = C.S2.sub(12288, 512)
    expb_sb = Buf(C.S2F.h, C.S2F.off + 6400, 1024)
    t_x0 = [mk.dma("sync", xred[r].ap(), C.rs_out[128 * r:128 * (r + 1), :],
                   waits=[t_rs]) for r in range(2)]
    xte_toks = []
    for r2 in range(2):
        xr = xred[r2]
        t_cl2 = mk.vop(lambda e, x_=xr: e.tensor_scalar(
            x_.ap(), x_.ap(), -33000.0, 65000.0, ALU.max, ALU.min),
            waits=[t_x0[r2]])
        t_s = mk.vop(lambda e, x_=xr: e.reduce_sum(lns[:, 4:5], x_.ap(),
                                                   axis=AX.X), waits=[t_cl2])
        t_sq = mk.vop(lambda e, x_=xr: e.tensor_tensor(sqb.ap(), x_.ap(),
                                                       x_.ap(), ALU.mult),
                      waits=[t_cl2])
        t_q = mk.vop(lambda e: e.reduce_sum(lns[:, 5:6], sqb.ap(), axis=AX.X),
                     waits=[t_sq])
        t_m = mk.vop(lambda e: e.tensor_scalar(
            lns[:, 6:7], lns[:, 4:5], 1.0 / D, None, ALU.mult),
            waits=[t_s], strict=True)
        t_m2 = mk.vop(lambda e: e.tensor_tensor(
            lns[:, 7:8], lns[:, 6:7], lns[:, 6:7], ALU.mult),
            waits=[t_m], strict=True)
        t_v = mk.vop(lambda e: e.scalar_tensor_tensor(
            lns[:, 5:6], lns[:, 5:6], 1.0 / D, lns[:, 7:8],
            ALU.mult, ALU.subtract), waits=[t_q, t_m2], strict=True)
        t_sd = mk.act(lns[:, 5:6], lns[:, 5:6], AF.Sqrt, bias=LN_EPS,
                      waits=[t_v])
        t_r = mk.vop(lambda e: e.reciprocal(lns[:, 5:6], lns[:, 5:6]),
                     waits=[t_sd])
        t_nm = mk.vop(lambda e: e.tensor_tensor(
            lns[:, 6:7], lns[:, 6:7], lns[:, 5:6], ALU.mult),
            waits=[t_r, t_m], strict=True)
        t_nm2 = mk.vop(lambda e: e.tensor_scalar(
            lns[:, 6:7], lns[:, 6:7], -1.0, None, ALU.mult),
            waits=[t_nm], strict=True)
        t_th = mk.act(xte.sub(1024 * r2, 1024).ap(), xr.ap(), AF.Tanh,
                      bias=lns[:, 6:7], scale=lns[:, 5:6],
                      waits=[t_nm2] + ([last_disp_mm] if r2 == 0 else []))
        xte_toks.append(t_th)
    tap(C, "xte", xte.ap(), xte_toks)
    if STOP_AFTER == "ffn0e1":
        C.tok_stream = C.tok_stream + xte_toks
        return
    tp_toks = []
    xteT3 = xteT.ap().rearrange("p (c r w) -> p c r w", r=2, w=128)
    bank_free = [t_ps4_free[0]]
    for r2 in range(2):
        t_t = None
        for c in range(NCD):
            t_t = mk.transp(C.PS4B[:, 128 * c:128 * (c + 1)],
                            xte[:, 1024 * r2 + 128 * c:1024 * r2 + 128 * (c + 1)],
                            C.IDB[:, :],
                            waits=[xte_toks[r2]] + (bank_free if c == 0 else []),
                            pub=(c == NCD - 1))
        t_cv = mk.vop(lambda e, rr=r2: e.tensor_copy(
            xteT3[:, :, rr, :],
            C.PS4B[:, 0:1024].rearrange("p (c w) -> p c w", w=128)),
            waits=[t_t])
        bank_free = [t_cv]
        tp_toks.append(t_cv)
    tap(C, "xteT", xteT.ap(), tp_toks)
    if STOP_AFTER == "ffn0e2":
        C.tok_stream = C.tok_stream + tp_toks
        return
    eslab = C.S2.sub(0, 8192)
    exw = C.expw.ap().rearrange("n (c p) m -> n c p m", p=128)
    seq = [(2 * r + j, hb) for r in range(4) for hb in range(2)
           for j in range(2)]
    es_toks = {}
    es_slot = {}
    es_cons = {0: [last_disp_mm], 1: [last_disp_mm]}

    def es_load(i):
        n_, hb_ = seq[i]
        sb_ = i % 2
        es_slot[(n_, hb_)] = sb_
        es_toks[(n_, hb_)] = mk.dma(
            "gpsimd", eslab[:, 4096 * sb_:4096 * (sb_ + 1)].rearrange(
                "p (c m) -> p c m", m=512),
            exw[n_, :, :, 512 * hb_:512 * (hb_ + 1)].transpose([1, 0, 2]),
            waits=es_cons[sb_])

    es_load(0)
    es_load(1)
    yt_dma = []
    bank_cons = {0: [], 1: []}
    t_eb_prev = None
    for r in range(4):
        bank = (C.PS4, C.PS5)[r % 2]
        t_eb = mk.dma("scalar", expb_sb.ap()[0:64, :],
                      C.expb_dev.ap()[r, :, :],
                      waits=[xte_toks[-1]] + ([t_eb_prev] if t_eb_prev else []))
        for hb in range(2):
            t_mm = None
            for j in range(2):
                n = 2 * r + j
                sb = es_slot[(n, hb)]
                for kc in range(NCD):
                    w = [es_toks[(n, hb)], tp_toks[-1]]
                    if kc == 0 and j == 0:
                        w += bank_cons[r % 2]
                    t_mm = mk.mm(bank[32 * j:32 * (j + 1), :],
                                 xteT[:, 256 * kc + 32 * (n % 8):
                                      256 * kc + 32 * (n % 8) + 32],
                                 eslab[:, 4096 * sb + 512 * kc:
                                       4096 * sb + 512 * (kc + 1)],
                                 start=(kc == 0), stop=(kc == NCD - 1),
                                 waits=w, tp=(0, 32 * j),
                                 pub=(kc == NCD - 1))
                i_cons = 4 * r + 2 * hb + j
                es_cons[sb] = [t_mm]
                if i_cons + 2 < 16:
                    es_load(i_cons + 2)
            t_ev = mk.vop(lambda e, b=bank, hb_=hb: e.tensor_tensor(
                ystmp.ap()[0:64, :], b[0:64, :],
                expb_sb.ap()[0:64, 512 * hb_:512 * (hb_ + 1)], ALU.add),
                waits=[t_mm, t_eb] + ([yt_dma[-1]] if yt_dma else []))
            t_d = mk.dma("sync", C.ag2_in[64 * r:64 * (r + 1),
                                          512 * hb:512 * (hb + 1)],
                         ystmp.ap()[0:64, :], waits=[t_ev])
            yt_dma.append(t_d)
            bank_cons[r % 2] = [t_ev]
        t_eb_prev = t_ev
    tap(C, "ytap", C.ag2_in.ap(), yt_dma)
    if STOP_AFTER == "ffn0e3":
        C.tok_stream = C.tok_stream + yt_dma
        return
    t_ag2 = mk.ccop("AllGather", ALU.bypass, [C.ag2_in.ap().opt()],
                    [C.ag2_out.ap().opt()], waits=yt_dma)
    if STOP_AFTER == "ffn0d":
        C.tok_stream = C.tok_stream + [t_ag2]
        return

    rhst = C.S2.sub(8192, 8192)
    ytsl = C.SCR.sub(0, 4096)
    ctile = C.S2.sub(0, 128)
    yt3 = C.ag2_out.ap().rearrange("(q p) m -> q p m", p=128)
    stream_toks = []
    ct_cons = [t_ag2]
    rh_busy = {}
    last_rh = []
    ys_cons = {0: [t_ag2, e_last], 1: [t_ag2, e_last]}
    ps_cons2 = [None, None]
    for tb in range(NTB):
        rhs_toks = {}
        for np_ in range(16):
            tts = []
            for ts in range(4):
                tc = 4 * tb + ts
                e3 = E[:, 2048 * tc + 128 * np_:2048 * tc + 128 * (np_ + 1)]
                rcb = sm(C, RC_)[:, 32 * tc:32 * (tc + 1)]
                t_ct = mk.vop(lambda e, s=e3, rb=rcb: e.tensor_tensor(
                    ctile.ap().rearrange("p (n q) -> p n q", q=32),
                    s.rearrange("p (n q) -> p n q", q=32),
                    rb.unsqueeze(1).broadcast_to([128, 4, 32]), ALU.mult),
                    waits=[rc_toks[tc]] + ct_cons)
                t_tp = mk.transp(C.PS4B[:, 128 * ts:128 * (ts + 1)],
                                 ctile.ap(), C.IDB[:, :],
                                 waits=[t_ct] + last_rh +
                                 (rh_busy.get(np_, []) if ts == 0 else []),
                                 pub=True)
                ct_cons = [t_tp]
                tts.append(t_tp)
            t_rh = mk.vop(lambda e, n=np_: e.tensor_copy(
                rhst[:, 512 * n:512 * (n + 1)], C.PS4B[:, 0:512]),
                waits=[tts[-1]])
            last_rh = [t_rh]
            rhs_toks[np_] = t_rh
            rh_busy[np_] = []
        for hc in range(NCD):
            yb = hc % 2
            t_ys = mk.dma("sync", ytsl[:, 2048 * yb:2048 * (yb + 1)].rearrange(
                "p (q m) -> p q m", m=128),
                yt3[:, :, 128 * hc:128 * (hc + 1)].transpose([1, 0, 2]),
                waits=ys_cons[yb])
            pq = hc % 2
            ps = (C.PS6, C.PS7)[pq]
            t_mm = None
            for np_ in range(16):
                w = [t_ys, rhs_toks[np_]]
                if np_ == 0 and ps_cons2[pq]:
                    w.append(ps_cons2[pq])
                t_mm = mk.mm(ps[:, :],
                             ytsl[:, 2048 * yb + 128 * np_:
                                  2048 * yb + 128 * (np_ + 1)],
                             rhst[:, 512 * np_:512 * (np_ + 1)],
                             start=(np_ == 0), stop=(np_ == 15), waits=w,
                             pub=(np_ == 15))
            ys_cons[yb] = [t_mm]
            if hc >= 6:
                for n2 in range(16):
                    rh_busy[n2] = [t_mm]
            t_e = mk.vop(lambda e, p=ps, m=hc, t=tb: e.tensor_tensor(
                XSc(C, m, t), p[:, :], XSc(C, m, t), ALU.add),
                waits=[t_mm] + C.tok_stream)
            ps_cons2[pq] = t_e
            stream_toks.append(t_e)
    tap(C, "rhstap", rhst.ap(), stream_toks)
    tap(C, "ytslt", ytsl.ap(), stream_toks)
    C.tok_stream = stream_toks


def qk_l1(C, src, wT, QF_out, gv_reg, gv_pat, do_stage, src_toks, ws_free):
    mk = C.mk
    ws1 = C.SCR.sub(0, 2048)
    kst = C.SCR.sub(2048, 2048)
    evf = [C.SCRF.sub(3072, 512), C.SCRF.sub(3584, 512)]
    krn = sm(C, KRN)
    wt3 = wT.ap().rearrange("(c p) m -> c p m", p=128)
    if gv_reg is not None:
        t_z = mk.vop(lambda e: e.memset(sm(C, gv_reg), 0.0),
                     waits=list(ws_free))
    else:
        t_z = None
    wtoks = {}
    cons = {0: list(ws_free), 1: list(ws_free)}

    def wslab(mc, buf):
        return mk.dma("gpsimd", ws1[:, 1024 * buf:1024 * (buf + 1)],
                      wt3[:, :, 128 * mc:128 * (mc + 1)].transpose([1, 0, 2]),
                      waits=cons[buf])

    wtoks[0] = wslab(0, 0)
    wtoks[1] = wslab(1, 1)
    qfree = [()] * 4
    sq_busy = [None, None]
    gacc = t_z
    prev_rep = None
    prev_nm = None
    kst_free = []
    stage_toks = []
    out_toks = []
    t_mm = None
    for mc in range(8):
        buf = mc % 2
        for tb in range(NTB):
            q = (mc * NTB + tb) % 4
            ps = C.PSB[:, 512 * q:512 * (q + 1)]
            for kc in range(NCD):
                w = [wtoks[mc], src_toks[tb * NCD + kc]]
                if kc == 0:
                    w += list(qfree[q])
                t_mm = mk.mm(ps, ws1[:, 1024 * buf + 128 * kc:
                                     1024 * buf + 128 * (kc + 1)],
                             slot_c(src, kc, tb), start=(kc == 0),
                             stop=(kc == NCD - 1), waits=w,
                             pub=(kc == NCD - 1))
            t_sq = mk.act(evf[buf].ap(), ps, AF.Square,
                          waits=[t_mm] + ([sq_busy[buf]] if sq_busy[buf] else []))
            t_pm = mk.mm(C.PS4[0:16, :], C.HPAT[:, 16 * mc:16 * (mc + 1)],
                         evf[buf].ap(), start=True, stop=True,
                         waits=[t_sq, gacc], pub=True)
            sq_busy[buf] = t_pm
            t_n1 = mk.act(krn[:, 0:512], C.PS4[0:16, :], AF.Sqrt,
                          waits=[t_pm] + ([prev_rep] if prev_rep else []))
            t_n2 = mk.vop(lambda e: e.tensor_scalar(
                krn[:, 0:512], krn[:, 0:512], 1e-12, None, ALU.max),
                waits=[t_n1])
            t_n3 = mk.vop(lambda e: e.reciprocal(krn[:, 0:512], krn[:, 0:512]),
                          waits=[t_n2])
            t_rep = mk.mm(C.PS5[:, :], C.RPF[:, 128 * mc:128 * (mc + 1)],
                          krn[:, 0:512], start=True, stop=True,
                          waits=[t_n3] + ([prev_nm] if prev_nm else []),
                          pub=True)
            prev_rep = t_rep
            if do_stage:
                dst = kst[:, 512 * tb:512 * (tb + 1)]
            else:
                dst = slot_c(QF_out, mc, tb)
            t_cp = mk.act(dst, ps, AF.Copy,
                          waits=[t_mm] +
                          (kst_free if do_stage and tb == 0 else []))
            t_nm = mk.vop(lambda e, d=dst: e.tensor_tensor(
                d, d, C.PS5[:, :], ALU.mult), waits=[t_rep, t_cp])
            prev_nm = t_nm
            if gv_reg is not None:
                t_gm = mk.mm(C.PS4[0:16, :], gv_pat[:, 16 * mc:16 * (mc + 1)],
                             dst, start=True, stop=True,
                             waits=[t_nm], pub=True)
                sq_busy[buf] = t_gm
                t_ga = mk.vop(lambda e, r=gv_reg, t=tb: e.tensor_tensor(
                    C.SM[r[0]:r[1], 512 * t:512 * (t + 1)],
                    C.SM[r[0]:r[1], 512 * t:512 * (t + 1)],
                    C.PS4[0:16, :], ALU.add), waits=[t_gm, gacc])
                gacc = t_ga
            qfree[q] = (t_nm, t_sq)
            out_toks.append(t_nm)
        if do_stage:
            koff = mc * 128 * 2048
            t_d = mk.dma("sync",
                         C.cc1_in[koff:koff + 128 * 2048].rearrange(
                             "(p t) -> p t", p=128), kst.ap(),
                         waits=[t_nm, gacc])
            kst_free = [t_d]
            stage_toks.append(t_d)
        if mc + 2 < 8:
            cons[buf] = [t_mm]
            wtoks[mc + 2] = wslab(mc + 2, buf)
    return out_toks, stage_toks, gacc


def att_block1(C):
    mk = C.mk
    xn1 = C.S0
    xn2i = C.S1
    entry = list(C.tok_stream) + C.tok_tern_done
    t_ln3 = ln_feat(C, lambda c, tb: XSc(C, c, tb),
                    lambda c, tb: slot_c(xn1, c, tb), entry)
    t_ln4 = ln_feat(C, lambda c, tb: slot_c(xn1, c, tb),
                    lambda c, tb: slot_c(xn2i, c, tb), t_ln3,
                    src_f32=False, lnwb=True)
    tap(C, "xn2i", xn2i.ap(), t_ln4)

    _, kstage, t_gkv = qk_l1(C, xn2i, C.wk, None, GKV, C.GKPAT, True,
                             t_ln4, entry)

    gkvt_toks = []
    t_prev = None
    for tc in range(NTC):
        t_t = mk.transp(C.PS4[:, 0:16],
                        sm(C, GKV)[:, 128 * tc:128 * (tc + 1)],
                        C.IDF[0:16, 0:16],
                        waits=[t_gkv] + ([t_prev] if t_prev else []), pub=True)
        t_c = mk.vop(lambda e, t=tc: e.tensor_copy(
            sm(C, GKVT)[:, 16 * t:16 * (t + 1)], C.PS4[:, 0:16]), waits=[t_t])
        t_prev = t_c
        gkvt_toks.append(t_c)

    vst = C.SCR.sub(4096, 2080)
    ws2 = C.S2.sub(0, 8192)
    vw3 = C.wv.ap().rearrange("(c p) m -> c p m", p=128)
    t_w = [mk.dma("gpsimd", ws2[:, 4096 * mb:4096 * (mb + 1)].rearrange(
        "p (c m) -> p c m", m=512),
        vw3[:, :, 512 * mb:512 * (mb + 1)].transpose([1, 0, 2]),
        waits=C.tok_pair_readers + C.tok_tern_done) for mb in range(2)]
    ones_ap = vst.ap().rearrange("p (b g pr w) -> p b g pr w", g=4, pr=2,
                                 w=65)[:, :, :, :, 64:65]
    t_ones = mk.vop(lambda e: e.memset(ones_ap, 1.0),
                    waits=entry)
    vst_free = [[t_ones], [t_ones]]
    qfree = [()] * 4
    vstage = []
    for tc in range(NTC):
        buf = tc % 2
        etoks = []
        for mb in range(2):
            q = (tc * 2 + mb) % 4
            ps = C.PSB[:, 512 * q:512 * (q + 1)]
            for kc in range(NCD):
                w = [t_w[mb], t_ln3[(tc // 4) * NCD + kc]]
                if kc == 0:
                    w += list(qfree[q])
                lhsT = xn1[:, 2048 * kc + 128 * tc:2048 * kc + 128 * (tc + 1)]
                t_mm = mk.mm(ps, lhsT,
                             ws2[:, 4096 * mb + 512 * kc:4096 * mb + 512 * (kc + 1)],
                             start=(kc == 0), stop=(kc == NCD - 1), waits=w,
                             pub=(kc == NCD - 1))
            dst4 = vst[:, 1040 * buf:1040 * (buf + 1)].rearrange(
                "p (hp pr w) -> p hp pr w", pr=2, w=65)[:, 4 * mb:4 * (mb + 1),
                                                        :, 0:64]
            src4 = ps.rearrange("p (hp pr d) -> p hp pr d", pr=2, d=64)
            g4 = sm(C, GKVT)[:, 16 * tc + 8 * mb:16 * tc + 8 * (mb + 1)] \
                .rearrange("p (hp pr) -> p hp pr", pr=2).unsqueeze(3) \
                .broadcast_to([128, 4, 2, 64])
            t_e = mk.vop(lambda e, d=dst4, s=src4, g=g4: e.scalar_tensor_tensor(
                d, s, 1.0, g, ALU.mult, ALU.mult),
                waits=[t_mm, gkvt_toks[tc]] +
                (vst_free[buf] if mb == 0 else []))
            etoks.append(t_e)
            qfree[q] = (t_e,)
        voff = KBE + tc * 128 * 1040
        t_d = mk.dma("sync",
                     C.cc1_in[voff:voff + 128 * 1040].rearrange(
                         "(p t) -> p t", p=128),
                     vst[:, 1040 * buf:1040 * (buf + 1)], waits=etoks)
        vst_free[buf] = [t_d]
        vstage.append(t_d)

    ws_free2 = [t for t_ in qfree for t in t_]
    q_toks, _, _ = qk_l1(C, xn2i, C.wq, C.S0, None, None, False,
                         t_ln4, ws_free2)

    ready = kstage + vstage + C.tok_pair_readers
    t_pair = gather_kv(C, ready, layer=1)

    y = C.S1
    att_toks = attention(C, layer=1, QF=C.S0, Y=y, t_pair=t_pair,
                         t_extra=q_toks[-1:])
    tap(C, "y1", y.ap(), att_toks)

    wo3 = C.wo.ap().rearrange("(c p) m -> c p m", p=128)

    def evac_p1(mc, tb, ps, t_mm):
        return mk.vop(lambda e, m=mc, t=tb, p=ps: e.scalar_tensor_tensor(
            XSc(C, m, t), p, C.BIASC[:, m:m + 1], XSc(C, m, t),
            ALU.add, ALU.add), waits=[t_mm] + C.tok_stream)

    ws_p1 = C.SCR.sub(0, 2048)
    C.tok_stream = matseq(C, wo3, lambda kc, tb: slot_c(y, kc, tb),
                          NCD, evac_p1, att_toks, ws_p1)


def ffn1(C):
    mk = C.mk
    xn = C.S0
    t_ln5 = ln_feat(C, lambda c, tb: XSc(C, c, tb),
                    lambda c, tb: slot_c(xn, c, tb), list(C.tok_stream))
    h = C.S1
    ws1 = C.SCR.sub(0, 2048)
    w2sl = C.S2.sub(0, 8192)
    w13 = C.w1q_d.ap().rearrange("(c p) m -> c p m", p=128)
    w23 = C.w2q_d.ap().rearrange("(c p) m -> c p m", p=128)
    evo = C.SCRF.sub(2560, 512)
    scol = sm(C, SCOL)
    stream_toks = []
    w1_cons = {0: C.tok_tern[0][-1:] + C.tok_stream[-1:],
               1: C.tok_tern[0][-1:] + C.tok_stream[-1:]}
    w2_cons = {0: C.tok_tern[1][-1:] + C.tok_stream[-1:],
               1: C.tok_tern[1][-1:] + C.tok_stream[-1:]}
    ps2_cons = [t_ln5[-1], t_ln5[-1]]
    h_toks = [()] * 32

    def w1slab(mc, buf):
        return mk.dma("sync", ws1[:, 1024 * buf:1024 * (buf + 1)].rearrange(
            "p (c m) -> p c m", m=128),
            w13[:, :, 128 * mc:128 * (mc + 1)].transpose([1, 0, 2]),
            waits=w1_cons[buf])

    for tb in range(NTB):
        w1toks = {0: w1slab(0, 0), 1: w1slab(1, 1)}
        qfree = [()] * 4
        t_mm = None
        for mc in range(32):
            buf = mc % 2
            q = mc % 4
            ps = C.PSB[:, 512 * q:512 * (q + 1)]
            for kc in range(NCD):
                w = [w1toks[mc], t_ln5[tb * NCD + kc]]
                if kc == 0:
                    w += list(qfree[q]) + list(h_toks[mc])
                t_mm = mk.mm(ps, ws1[:, 1024 * buf + 128 * kc:
                                     1024 * buf + 128 * (kc + 1)],
                             slot_c(xn, kc, tb), start=(kc == 0),
                             stop=(kc == NCD - 1), waits=w,
                             pub=(kc == NCD - 1))
            t_g = mk.act(h[:, 512 * mc:512 * (mc + 1)], ps, AF.Gelu,
                         bias=C.BIASC[:, 32 + mc:33 + mc],
                         waits=[t_mm] + C.tok_tern_sc)
            qfree[q] = (t_g,)
            h_toks[mc] = (t_g,)
            w1_cons[buf] = [t_mm]
            if mc + 2 < 32:
                w1toks[mc + 2] = w1slab(mc + 2, buf)
        w2toks = {}
        for mc2 in range(NCD):
            buf = mc2 % 2
            w2toks[mc2] = mk.dma(
                "sync", w2sl[:, 4096 * buf:4096 * (buf + 1)].rearrange(
                    "p (c m) -> p c m", m=128),
                w23[:, :, 128 * mc2:128 * (mc2 + 1)].transpose([1, 0, 2]),
                waits=w2_cons[buf])
            pq = mc2 % 2
            ps = (C.PS6, C.PS7)[pq]
            for kc in range(32):
                w = [w2toks[mc2]] + list(h_toks[kc])
                if kc == 0 and ps2_cons[pq]:
                    w.append(ps2_cons[pq])
                t_mm = mk.mm(ps[:, :], w2sl[:, 4096 * buf + 128 * kc:
                                            4096 * buf + 128 * (kc + 1)],
                             h[:, 512 * kc:512 * (kc + 1)], start=(kc == 0),
                             stop=(kc == 31), waits=w, pub=(kc == 31))
            t_sc = mk.vop(lambda e, p=ps, m=mc2: e.tensor_scalar(
                evo.ap(), p[:, :], 1.0, C.BIASC[:, 24 + m:25 + m],
                ALU.mult, ALU.add), waits=[t_mm] + C.tok_tern_sc)
            t_e = mk.vop(lambda e, m=mc2, t=tb: e.tensor_tensor(
                XSc(C, m, t), evo.ap(), XSc(C, m, t), ALU.add),
                waits=[t_sc] + C.tok_stream)
            ps2_cons[pq] = t_e
            w2_cons[buf] = [t_mm]
            stream_toks.append(t_e)
        h_toks = [(t_mm,)] * 32
    tap(C, "w1qtap", C.w1q_d.ap(), C.tok_tern[0][-1:])
    tap(C, "w2qtap", C.w2q_d.ap(), C.tok_tern[1][-1:])
    tap(C, "htap", h.ap(), stream_toks)
    tap(C, "scltap", C.SM[0:128, 3616:3632], stream_toks)
    C.tok_stream = stream_toks


def _local_token_map(h):
    idx = []
    for i in range(4):
        b = 2 * i + h
        idx.extend(range(b * TB, (b + 1) * TB))
    return np.array(idx)


def _gathered_abs():
    g = np.arange(TSEQ)
    r, j = g // TLOC, g % TLOC
    blk, off = j // TB, j % TB
    return (2 * blk + r) * TB + off


def _build_masks(h):
    m = np.zeros((4, 8, 128, TB), np.float32)
    gpos = _gathered_abs()
    for i in range(4):
        qabs = (2 * i + h) * TB + np.arange(TB)
        for jj in range(8):
            G = [i, 4 + i][jj // 4]
            s0 = G * TB + (jj % 4) * 128
            sabs = gpos[s0:s0 + 128]
            m[i, jj] = (sabs[:, None] <= qabs[None, :]).astype(np.float32)
    return m.astype(ml_dtypes.bfloat16)


_PROG_CACHE = {}


def _get_prog(taps=()):
    key = (STOP_AFTER, tuple(sorted(taps)))
    if key not in _PROG_CACHE:
        _PROG_CACHE[key] = build_program(tuple(sorted(taps)))
    return _PROG_CACHE[key]


def make_inputs(full):
    f32 = np.float32
    bf = ml_dtypes.bfloat16
    x = np.asarray(full["L0_attn_x"], f32)
    expw_all = np.asarray(full["L0_ffn_expert_w"], f32)
    expb_all = np.asarray(full["L0_ffn_expert_b"], f32)
    gq = np.asarray(full["L1_attn_gq"], f32)
    gk = np.asarray(full["L1_attn_gk"], f32)
    headpat = np.zeros((128, 8 * 16), f32)
    gqpat = np.zeros((128, 8 * 16), f32)
    gkpat = np.zeros((128, 8 * 16), f32)
    for c in range(8):
        for d in range(128):
            hh = 2 * c + d // 64
            headpat[d, 16 * c + hh] = 1.0
            gqpat[d, 16 * c + hh] = gq[d % 64]
            gkpat[d, 16 * c + hh] = gk[d % 64]
    gqb = np.zeros((128, 256), f32)
    for par in range(2):
        for d in range(64):
            gqb[64 * par + d, 128 * par:128 * (par + 1)] = gq[d]
    reppatF = np.zeros((16, 8 * 128), f32)
    for c in range(8):
        for m_ in range(128):
            reppatF[2 * c + m_ // 64, 128 * c + m_] = 1.0
    ident = np.eye(128, dtype=f32)

    shared = {
        "qkvT": np.ascontiguousarray(np.asarray(full["L0_attn_W_qkv"], f32).T),
        "projT": np.ascontiguousarray(np.asarray(full["L0_attn_W_proj"], f32).T),
        "phiT": np.ascontiguousarray(np.asarray(full["L0_ffn_phi_w"], f32).T),
        "phib_row": np.asarray(full["L0_ffn_phi_b"], f32).reshape(1, 2048),
        "wq": np.ascontiguousarray(np.asarray(full["L1_attn_Wq"], f32)),
        "wk": np.ascontiguousarray(np.asarray(full["L1_attn_Wk"], f32)),
        "wv": np.ascontiguousarray(np.asarray(full["L1_attn_Wv"], f32)),
        "wo": np.ascontiguousarray(np.asarray(full["L1_attn_Wo"], f32)),
        "bo_c": np.ascontiguousarray(
            np.asarray(full["L1_attn_bo"], f32).reshape(NCD, 128).T),
        "lnw_c": np.ascontiguousarray(
            np.asarray(full["L1_attn_ln_w"], f32).reshape(NCD, 128).T),
        "lnb_c": np.ascontiguousarray(
            np.asarray(full["L1_attn_ln_b"], f32).reshape(NCD, 128).T),
        "w1T": np.ascontiguousarray(np.asarray(full["L1_ffn_W1"], f32).T),
        "b1_c": np.ascontiguousarray(
            np.asarray(full["L1_ffn_b1"], f32).reshape(32, 128).T),
        "w2T": np.ascontiguousarray(np.asarray(full["L1_ffn_W2"], f32).T),
        "b2_c": np.ascontiguousarray(
            np.asarray(full["L1_ffn_b2"], f32).reshape(NCD, 128).T),
        "ident_b": ident.astype(bf), "ident_f": ident,
        "headpat": headpat,
        "gqpat": gqpat.astype(bf), "gkpat": gkpat.astype(bf),
        "reppatF": reppatF,
        "ones_r": np.ones((128, 128), f32),
        "gqb": gqb.astype(bf),
        "ones_rb": np.ones((1, 128), f32).astype(bf),
        "ones_c": np.ones((128, 1), f32),
        "ones_cb": np.ones((128, 1), f32).astype(bf),
    }
    in_maps = []
    for c in range(NCORES):
        b, h = divmod(c, 2)
        expb = np.repeat(expb_all[8 * c:8 * c + 8], 32, axis=0)
        m = dict(shared)
        m["xT"] = np.ascontiguousarray(x[b][_local_token_map(h)].T)
        m["expw"] = np.ascontiguousarray(expw_all[8 * c:8 * c + 8])
        m["expb_dev"] = np.ascontiguousarray(expb.reshape(4, 64, D))
        m["masks_in"] = _build_masks(h)
        in_maps.append(m)
    return in_maps


def kernel(**inputs):
    nc = _get_prog()
    in_maps = make_inputs(inputs)
    res = run_bass_kernel_spmd(nc, in_maps, core_ids=list(range(NCORES)))
    out = np.zeros((4, TSEQ, D), np.float32)
    for c in range(NCORES):
        b, h = divmod(c, 2)
        out[b][_local_token_map(h)] = res.results[c]["out_xT"].T
    return out
